# version 12
# baseline (speedup 1.0000x reference)
import contextlib
import numpy as np
import concourse.bass as bass
import concourse.mybir as mybir
from concourse.bass_utils import run_bass_kernel_spmd

F32 = mybir.dt.float32
BF16 = mybir.dt.bfloat16
I32 = mybir.dt.int32
AF = mybir.ActivationFunctionType
ALU = mybir.AluOpType

D = 1024
NB = 8
TBLK = 2048
XC = TBLK + 3
KC = TBLK + 2
HW = 1028
M30 = 30.0
EPS = 1e-5

PH1_XT = [(0, 512), (512, 1024), (1024, 1536), (1536, 1800), (1800, 2051)]
PH3_T = {0: [(0, 512), (512, 1023)], 1: [(1023, 1535), (1535, 1793), (1793, 2050)]}

_TAPS = []
_PARANOID = {"small", "ph34b", "ph34d"}
_PARANOID_ALL = False
_CACHE = {}


class Prog:
    def __init__(self, nc):
        self.nc = nc
        self.h = {"pe": nc.tensor, "dve": nc.vector, "act": nc.scalar,
                  "pool": nc.gpsimd, "sp": nc.sync}
        self.sem = {}
        self.cnt = {}
        self.last = {}
        self.marked = {}
        self.waited = {}

    def init_sems(self, ctx):
        for name in ("m_pe", "m_dve", "m_act", "m_pool", "q_sp", "q_gp", "q_out"):
            self.sem[name] = ctx.enter_context(self.nc.semaphore(name))
            self.cnt[name] = 0

    paranoid = False
    paranoid_dma = False

    def emit(self, eng, instr):
        self.last[eng] = instr
        self.marked[eng] = False
        if self.paranoid and eng in ("dve", "act", "pool"):
            tok = self.mark(eng)
            for other in ("dve", "act", "pool", "pe", "sp"):
                self.wait(other, tok)
        return instr

    def mark(self, eng):
        s = "m_" + eng
        if not self.marked.get(eng):
            assert eng in self.last, f"mark({eng}) with nothing emitted"
            self.cnt[s] += 1
            self.last[eng].then_inc(self.sem[s], 1)
            self.marked[eng] = True
        return (s, self.cnt[s])

    def wait(self, waiter, tok):
        if tok is None:
            return
        s, v = tok
        key = (waiter, s)
        if self.waited.get(key, -1) >= v:
            return
        self.h[waiter].wait_ge(self.sem[s], v)
        self.waited[key] = v

    def dma(self, queue, out, in_, eng=None):
        if eng is None:
            eng = {"q_sp": "sp", "q_gp": "pool", "q_out": "sp"}[queue]
        if self.paranoid:
            for e in ("dve", "act", "pool"):
                if e in self.last:
                    self.wait(eng, self.mark(e))
        ins = self.h[eng].dma_start(out=out, in_=in_)
        self.cnt[queue] += 16
        ins.then_inc(self.sem[queue], 16)
        tok = (queue, self.cnt[queue])
        if self.paranoid:
            self.h[eng].wait_ge(self.sem[queue], self.cnt[queue])
            self.waited[(eng, queue)] = self.cnt[queue]
            for e in ("dve", "act", "pool", "pe"):
                self.wait(e, tok)
        elif self.paranoid_dma:
            self.h[eng].wait_ge(self.sem[queue], self.cnt[queue])
            self.waited[(eng, queue)] = self.cnt[queue]
        return tok


def build_program(taps=()):
    nc = bass.Bass("TRN2", target_bir_lowering=False)
    taps = set(taps)

    def din(name, shape, dt=F32):
        return nc.dram_tensor(name, shape, dt, kind="ExternalInput").ap()

    E = {}
    E["xfm"] = din("xfm", [NB, 128, XC])
    E["xred"] = din("xred", [NB, 128, XC])
    E["vpk"] = din("vpk", [128, NB, 13])
    E["csel"] = din("csel", [128, 16])
    E["cpar"] = din("cpar", [1, 2], I32)
    for nm, shp in [("wk", [D, D]), ("wv", [D, D]), ("wr", [D, D]), ("wo", [D, D]),
                    ("fwk", [D, 4 * D]), ("fwrec", [D, D]), ("fwv", [4 * D, D]),
                    ("cwr", [D, D]), ("cwk", [D, D]), ("cwv", [D, D])]:
        E[nm + "_d"] = din(nm, shp)
    E["out_d"] = nc.dram_tensor("out", [NB, 128, TBLK], F32, kind="ExternalOutput").ap()

    E["tap_d"] = {}
    for tname in taps:
        shp = {"sden": [NB, 128, HW], "snum": [NB, 128, HW],
               "sr": [NB, 128, KC], "carry": [2, 128, NB],
               "a": [NB, 128, KC], "sums": [4, 128, NB],
               "zattn": [NB, 128, KC], "xn2": [NB, 128, KC], "z1t": [NB, 128, KC],
               "at3": [NB, 128, KC], "cmb": [NB, 4, 128, 1040], "dv": [128, NB, 8],
               "za2": [NB, 128, KC], "x2t": [NB, 128, KC]}[tname]
        dt = BF16 if tname in ("sr", "a", "at3") else F32
        E["tap_d"][tname] = nc.dram_tensor("tap_" + tname, shp, dt,
                                           kind="ExternalOutput").ap()

    for nm in ("wk", "wv", "wr", "wo", "fwrec", "cwr", "cwk", "cwv"):
        E[nm + "_b"] = nc.dram_tensor(nm + "b", [128, 2, NB, 4, 128], BF16).ap()
    E["fwk_b"] = nc.dram_tensor("fwkb", [128, 8, NB, 4, 128], BF16).ap()
    E["fwv_b"] = nc.dram_tensor("fwvb", [128, 4, 2, NB, 4, 128], BF16).ap()

    E["spill"] = nc.dram_tensor("spill", [2, NB, 128, 1023], F32).ap()
    E["sr_hbm"] = nc.dram_tensor("sr_hbm", [NB, 128, KC], BF16).ap()
    E["a_hbm"] = nc.dram_tensor("a_hbm", [NB, 128, KC], BF16).ap()
    E["shx"] = nc.dram_tensor("shx", [2, 4, 128, NB], F32, addr_space="Shared").ap()
    E["shf"] = nc.dram_tensor("shf", [2, 1], I32, addr_space="Shared").ap()

    ctx = contextlib.ExitStack()
    with ctx:
        def sb(name, shape, dt=F32):
            t = ctx.enter_context(nc.sbuf_tensor(name, shape, dt))
            E[name] = t
            return t

        def psm(name, shape):
            t = ctx.enter_context(nc.psum_tensor(name, shape, F32))
            E[name] = t
            return t

        sb("Sd", [128, NB, HW])
        sb("Sn", [128, NB, HW])
        sb("xt", [128, NB, 512])
        sb("scr", [128, NB, 512])
        sb("scr2", [128, NB, 512])
        sb("mixa", [128, NB, 512], BF16)
        sb("mixb", [128, NB, 512], BF16)
        sb("wsA", [128, NB, 4, 128], BF16)
        sb("wsB", [128, NB, 4, 128], BF16)
        sb("fka", [128, NB, 512], BF16)
        sb("at3", [128, NB, 512], BF16)
        sb("srt", [128, 2, 520], BF16)
        sb("srh", [128, 2, 1040], BF16)
        sb("aw", [128, 2, 1040], BF16)
        sb("big", [128, 2, 1056])
        sb("rows", [1, 5, 520])
        sb("rho_sv", [1, XC])
        sb("mrh_sv", [1, XC])
        sb("xnprev", [128, NB, 1])
        sb("statec", [128, NB, 2])
        sb("hsave", [128, NB, 2])
        sb("vecs", [128, NB, 13])
        sb("dvec", [128, NB, 8])
        sb("cselt", [128, 16])
        sb("pairin", [128, 4, NB])
        sb("sumown", [128, 2, NB])
        sb("sumred", [128, 2, NB])
        sb("carry", [128, 2, NB])
        sb("ones_d", [128, 1])
        sb("onesr", [1, 128])
        sb("cm30", [128, 1])
        sb("cmask", [128, 2])
        sb("zer2", [128, 2])
        sb("cmaski", [128, 1], I32)
        sb("ceps", [128, 1])
        sb("spoll", [1, 1], I32)
        sb("spar", [1, 2], I32)
        sb("sflag", [1, 1], I32)

        psm("PA", [128, 512])
        psm("PB", [128, 512])
        psm("PC", [128, 512])
        psm("PD", [128, 512])
        psm("PSTAT", [128, 1024])
        psm("PBC", [128, 1024])

        P = Prog(nc)
        P.init_sems(ctx)

        with nc.Block():
            _emit_all(P, E)

    return nc, E["tap_d"]



def _emit_all(P, E):
    dve, act, pool = P.h["dve"], P.h["act"], P.h["pool"]

    P.tok_cast = {}
    for nm in ("wk", "wv", "wr", "wo", "fwrec", "cwr", "cwk", "cwv"):
        P.tok_cast[nm] = P.dma(
            "q_gp", E[nm + "_b"],
            E[nm + "_d"].rearrange("(a p) (h f m) -> p h a f m", p=128, h=2, f=4, m=128))
    P.tok_cast["fwk"] = P.dma(
        "q_gp", E["fwk_b"],
        E["fwk_d"].rearrange("(a p) (g f m) -> p g a f m", p=128, g=8, f=4, m=128))
    for q_ in range(4):
        for h_ in range(2):
            P.tok_cast["fwv"] = P.dma(
                "q_gp", E["fwv_b"][:, q_, h_],
                E["fwv_d"][1024 * q_:1024 * (q_ + 1),
                           512 * h_:512 * (h_ + 1)].rearrange(
                    "(k p) fm -> p k fm", p=128))

    P.emit("pool", P.h["pool"].memset(E["spoll"][:], 0))
    tokz = P.mark("pool")
    P.dma("q_gp", E["shf"][0:1, :], E["spoll"][:])
    P.dma("q_gp", E["shf"][1:2, :], E["spoll"][:])
    tok_vec = P.dma("q_sp", E["vecs"][:], E["vpk"])
    tok_csel = P.dma("q_sp", E["cselt"][:], E["csel"])
    P.dma("q_sp", E["spar"][:], E["cpar"])

    P.emit("dve", P.h["dve"].memset(E["zer2"][:], 0.0))
    for t, v in [("ones_d", 1.0 / D), ("onesr", 1.0), ("cm30", -M30),
                 ("ceps", EPS)]:
        P.emit("dve", dve.memset(E[t][:], v))
    for t in ("Sd", "Sn", "statec", "xnprev", "hsave"):
        P.emit("dve", dve.memset(E[t][:], 0.0))

    dv = E["dvec"]
    P.wait("act", tok_vec)
    P.emit("act", act.activation(out=dv[:, :, 0:1], in_=E["vecs"][:, :, 9:10], func=AF.Exp))
    P.wait("dve", P.mark("act"))
    P.emit("dve", dve.tensor_scalar(out=dv[:, :, 1:2], in0=dv[:, :, 0:1],
                                    scalar1=-1.0, scalar2=None, op0=ALU.mult))
    P.wait("act", P.mark("dve"))
    P.emit("act", act.activation(out=dv[:, :, 2:3], in_=dv[:, :, 1:2], func=AF.Exp))
    P.emit("act", act.activation(out=dv[:, :, 3:4], in_=E["vecs"][:, :, 10:11], func=AF.Exp))
    P.wait("dve", P.mark("act"))
    P.emit("dve", dve.tensor_mul(dv[:, :, 7:8], dv[:, :, 3:4], dv[:, :, 2:3]))
    P.wait("dve", P.mark("dve"))
    P.emit("dve", dve.tensor_scalar(out=dv[:, :, 4:5], in0=dv[:, :, 7:8],
                                    scalar1=-1.0, scalar2=1.0, op0=ALU.mult, op1=ALU.add))
    P.wait("dve", P.mark("dve"))
    P.emit("dve", dve.tensor_scalar(out=dv[:, :, 5:6], in0=dv[:, :, 3:4],
                                    scalar1=float(2.0 ** 52), scalar2=None, op0=ALU.mult))
    P.emit("dve", dve.tensor_scalar(out=dv[:, :, 6:7], in0=dv[:, :, 4:5],
                                    scalar1=float(2.0 ** 52), scalar2=None, op0=ALU.mult))
    P.wait("dve", P.mark("dve"))
    P.wait("dve", tok_csel)
    P.emit("dve", dve.tensor_copy(E["cmask"][:, 0:1], E["cselt"][:, 0:1]))
    P.emit("dve", dve.tensor_copy(E["cmask"][:, 1:2], E["cselt"][:, 1:2]))
    P.emit("dve", dve.tensor_copy(E["cmaski"][:], E["cselt"][:, 0:1]))
    P.tok_dvec = P.mark("dve")
    P.paranoid = ("ph1" in _PARANOID) or _PARANOID_ALL
    P.paranoid_dma = ("ph1dma" in _PARANOID) or _PARANOID_ALL

    st = _St()
    for tau, (ax, bx) in enumerate(PH1_XT):
        _ph1_tile(P, E, st, tau, ax, bx, red=False)
    P.wait("dve", st.tok_scanprev)
    P.emit("dve", dve.tensor_copy(E["sumown"][:, 0, :], E["Sd"][:, :, 1025]))
    P.emit("dve", dve.tensor_copy(E["sumown"][:, 1, :], E["Sn"][:, :, 1025]))
    tok_sumown = P.mark("dve")

    str_ = _St()
    str_.inherit(st)
    for tau, (ax, bx) in enumerate(PH1_XT):
        _ph1_tile(P, E, str_, tau, ax, bx, red=True)
    tok_sumred = P.mark("dve")
    P.st_after_ph1 = str_

    if "sums" in E["tap_d"]:
        P.wait("sp", tok_sumown)
        P.wait("sp", tok_sumred)
        P.dma("q_out", E["tap_d"]["sums"][0:2].rearrange("c p b -> p c b"),
              E["sumown"][:])
        P.dma("q_out", E["tap_d"]["sums"][2:4].rearrange("c p b -> p c b"),
              E["sumred"][:])

    P.paranoid = ("small" in _PARANOID) or _PARANOID_ALL
    P.paranoid_dma = P.paranoid
    _exchange(P, E, tok_sumown, tok_sumred)
    _carry(P, E)
    if "carry" in E["tap_d"]:
        P.wait("sp", P.mark("dve"))
        P.dma("q_out", E["tap_d"]["carry"].rearrange("c p b -> p c b"), E["carry"][:])

    P.paranoid = True
    if "dv" in E["tap_d"]:
        P.wait("sp", P.mark("dve"))
        tz = P.dma("q_out", E["tap_d"]["dv"], E["dvec"][:])
        P.h["sp"].wait_ge(P.sem["q_out"], tz[1])
    _correct_h1(P, E)
    if "sden" in E["tap_d"]:
        P.wait("sp", P.mark("dve"))
        P.dma("q_out", E["tap_d"]["sden"].rearrange("a p c -> p a c"), E["Sd"][:])
        tsnap = P.dma("q_out", E["tap_d"]["snum"].rearrange("a p c -> p a c"), E["Sn"][:])
        P.h["sp"].wait_ge(P.sem["q_out"], tsnap[1])
    _combine(P, E, half=1)
    _h0_scan(P, E)
    _combine(P, E, half=0)
    P.paranoid = ("ph34" in _PARANOID) or _PARANOID_ALL
    P.paranoid_dma = ("ph34dma" in _PARANOID) or P.paranoid or _PARANOID_ALL

    if "sr" in E["tap_d"]:
        P.wait("sp", P.tok_sr_done)
        P.dma("q_out", E["tap_d"]["sr"], E["sr_hbm"])
    if "a" in E["tap_d"]:
        P.wait("sp", P.tok_a_done)
        P.dma("q_out", E["tap_d"]["a"], E["a_hbm"])

    for half in (0, 1):
        for (ka, kb) in PH3_T[half]:
            _ph34_tile(P, E, ka, kb)

    P.h["sp"].wait_ge(P.sem["q_out"], P.cnt["q_out"])
    P.h["sp"].wait_ge(P.sem["q_sp"], P.cnt["q_sp"])


class _St:
    def __init__(self):
        self.tok_xn_prev = None
        self.tok_tile_consumed = None
        self.tok_spill = None
        self.tok_scanprev = None
        self.tok_rows_prev = None
        self.tok_bc_consumed = None
        self.toks_bank = {}
        self.tok_sq_war = None
        self.tok_stats_src = None
        self.tok_ws = {}
        self.tok_ws_war = {}
        self.tok_srt = {}

    def inherit(self, o):
        self.tok_xn_prev = o.tok_xn_prev
        self.tok_tile_consumed = o.tok_tile_consumed
        self.tok_rows_prev = o.tok_rows_prev
        self.tok_bc_consumed = o.tok_bc_consumed
        self.toks_bank = dict(o.toks_bank)
        self.tok_sq_war = o.tok_sq_war
        self.tok_ws_war = dict(o.tok_ws_war)
        self.tok_srt = dict(o.tok_srt)


def _load_wgrp(P, E, st, slot, grp_ap, tok_cast):
    P.wait("sp", st.tok_ws_war.get(slot))
    P.wait("sp", tok_cast)
    kbn = grp_ap.shape[1]
    tok = P.dma("q_sp", E[slot][:, 0:kbn, :, :], grp_ap)
    st.tok_ws[slot] = tok
    return tok


def _stats(P, E, xap, wx, st):
    pe, dve, act, pool = P.h["pe"], P.h["dve"], P.h["act"], P.h["pool"]
    R = E["rows"]
    if st.tok_sq_war is not None:
        P.wait("pool", st.tok_sq_war)
    if getattr(st, "tok_scr2_dve", None) is not None:
        P.wait("pool", st.tok_scr2_dve)
    P.wait("pool", st.tok_stats_src)
    P.emit("pool", pool.tensor_mul(E["scr2"][:, :, 0:wx], xap, xap))
    tok_sq = P.mark("pool")
    if st.tok_rows_prev is not None:
        P.wait("pe", st.tok_rows_prev)
    P.wait("pe", st.tok_stats_src)
    for kb in range(NB):
        P.emit("pe", pe.matmul(E["PSTAT"][0:1, 0:wx], E["ones_d"][:],
                               xap[:, kb, 0:wx], start=(kb == 0), stop=(kb == NB - 1)))
    P.wait("pe", tok_sq)
    for kb in range(NB):
        P.emit("pe", pe.matmul(E["PSTAT"][0:1, 512:512 + wx], E["ones_d"][:],
                               E["scr2"][:, kb, 0:wx], start=(kb == 0), stop=(kb == NB - 1)))
    tok_mm = P.mark("pe")
    st.tok_sq_war = tok_mm
    P.wait("act", tok_mm)
    P.emit("act", act.activation(out=R[:, 0, 0:wx], in_=E["PSTAT"][0:1, 0:wx], func=AF.Copy))
    P.emit("act", act.activation(out=R[:, 3, 0:wx], in_=E["PSTAT"][0:1, 0:wx], func=AF.Square))
    tok_r = P.mark("act")
    P.wait("dve", tok_mm)
    P.wait("dve", tok_r)
    P.emit("dve", dve.tensor_sub(R[:, 1, 0:wx], E["PSTAT"][0:1, 512:512 + wx], R[:, 3, 0:wx]))
    tok_var = P.mark("dve")
    st.tok_rows_prev = tok_var
    return tok_var


def _rsqrt_row(P, E, src_row, dst_row, wx, tok_dep):
    act = P.h["act"]
    R = E["rows"]
    P.wait("act", tok_dep)
    P.emit("act", act.activation(out=R[:, 4, 0:wx], in_=src_row, func=AF.Ln,
                                 bias=E["ceps"][0:1, :], scale=1.0))
    P.wait("act", P.mark("act"))
    P.emit("act", act.activation(out=dst_row, in_=R[:, 4, 0:wx], func=AF.Exp, scale=-0.5))
    return P.mark("act")


def _bcast(P, E, row_ap, wx, pcol, tok_row, war_tok):
    pe = P.h["pe"]
    P.wait("pe", tok_row)
    if war_tok is not None:
        P.wait("pe", war_tok)
    P.emit("pe", pe.matmul(E["PBC"][:, pcol:pcol + wx], E["onesr"][:],
                           row_ap[:, 0:wx], start=True, stop=True))
    return P.mark("pe")


def _ph1_tile(P, E, st, tau, ax, bx, red):
    pe, dve, act, pool = P.h["pe"], P.h["dve"], P.h["act"], P.h["pool"]
    wx = bx - ax
    ak = ax - 1 if ax > 0 else 0
    bk = bx - 1
    wk_ = bk - ak
    xsrc = E["xred"] if red else E["xfm"]
    V = E["vecs"]

    if red:
        Dst_d, Dst_n = E["xt"], E["scr"]
        lc = 0
    else:
        Dst_d, Dst_n = E["Sd"], E["Sn"]
        lc = (1 + ak) if tau <= 1 else (ak - 1023 + 1)

    if st.tok_tile_consumed is not None:
        P.wait("sp", st.tok_tile_consumed)
    if st.tok_scanprev is not None:
        P.wait("sp", st.tok_scanprev)
    if st.tok_xn_prev is not None:
        P.wait("sp", st.tok_xn_prev)
    tok_x = P.dma("q_sp", E["xt"][:, :, 0:wx],
                  xsrc[:, :, ax:bx].rearrange("a p w -> p a w"))
    st.tok_stats_src = tok_x

    tok_var = _stats(P, E, E["xt"][:, :, 0:wx], wx, st)
    R = E["rows"]
    tok_rho = _rsqrt_row(P, E, R[:, 1, 0:wx], R[:, 2, 0:wx], wx, tok_var)
    P.wait("dve", tok_rho)
    P.emit("dve", dve.tensor_mul(R[:, 3, 0:wx], R[:, 0, 0:wx], R[:, 2, 0:wx]))
    if not red:
        P.emit("dve", dve.tensor_copy(E["rho_sv"][:, ax:bx], R[:, 2, 0:wx]))
        P.emit("dve", dve.tensor_copy(E["mrh_sv"][:, ax:bx], R[:, 3, 0:wx]))
    P.emit("dve", dve.tensor_mul(R[:, 0, 0:wx], R[:, 2, 0:wx], R[:, 2, 0:wx]))
    P.emit("dve", dve.tensor_mul(R[:, 0, 0:wx], R[:, 1, 0:wx], R[:, 0, 0:wx]))
    tok_vz = P.mark("dve")
    tok_rho2 = _rsqrt_row(P, E, R[:, 0, 0:wx], R[:, 1, 0:wx], wx, tok_vz)
    P.wait("dve", tok_rho2)
    P.emit("dve", dve.tensor_mul(R[:, 0, 0:wx], R[:, 2, 0:wx], R[:, 1, 0:wx]))
    P.emit("dve", dve.tensor_mul(R[:, 3, 0:wx], R[:, 3, 0:wx], R[:, 1, 0:wx]))
    tok_R = P.mark("dve")
    st.tok_rows_prev = tok_R

    tok_bc = _bcast(P, E, R[:, 0, :], wx, 0, tok_R, st.tok_bc_consumed)
    tok_bc2 = _bcast(P, E, R[:, 3, :], wx, 512, tok_R, st.tok_bc_consumed)

    P.wait("dve", tok_bc)
    P.wait("dve", tok_bc2)
    P.wait("dve", P.tok_dvec)
    for kb in range(NB):
        P.emit("dve", dve.tensor_mul(E["xt"][:, kb, 0:wx], E["xt"][:, kb, 0:wx],
                                     E["PBC"][:, 0:wx]))
        P.emit("dve", dve.tensor_sub(E["xt"][:, kb, 0:wx], E["xt"][:, kb, 0:wx],
                                     E["PBC"][:, 512:512 + wx]))
        P.emit("dve", dve.tensor_scalar(out=E["xt"][:, kb, 0:wx], in0=E["xt"][:, kb, 0:wx],
                                        scalar1=V[:, kb, 0:1], scalar2=V[:, kb, 1:2],
                                        op0=ALU.mult, op1=ALU.add))
    if tau == 0:
        mcol = E["cmask"][:, 1:2] if red else E["cmask"][:, 0:1]
        for kb in range(NB):
            P.emit("dve", dve.tensor_scalar(out=E["xt"][:, kb, 0:3], in0=E["xt"][:, kb, 0:3],
                                            scalar1=mcol, scalar2=None, op0=ALU.mult))
    tok_xn = P.mark("dve")
    st.tok_bc_consumed = tok_xn

    def emit_mix(Mt, vidx, war_tok):
        if war_tok is not None:
            P.wait("dve", war_tok)
        if st.tok_xn_prev is not None:
            P.wait("dve", st.tok_xn_prev)
        M = E[Mt]
        if tau == 0:
            P.emit("dve", dve.tensor_sub(E["scr2"][:, :, 0:wk_],
                                         E["xt"][:, :, 1:wk_ + 1], E["xt"][:, :, 0:wk_]))
            for kb in range(NB):
                P.emit("dve", dve.scalar_tensor_tensor(
                    out=M[:, kb, 0:wk_], in0=E["scr2"][:, kb, 0:wk_],
                    scalar=V[:, kb, vidx:vidx + 1],
                    in1=E["xt"][:, kb, 0:wk_], op0=ALU.mult, op1=ALU.add))
        else:
            P.emit("dve", dve.tensor_sub(E["scr2"][:, :, 0:1],
                                         E["xt"][:, :, 0:1], E["xnprev"][:]))
            P.emit("dve", dve.tensor_sub(E["scr2"][:, :, 1:wk_],
                                         E["xt"][:, :, 1:wk_], E["xt"][:, :, 0:wk_ - 1]))
            for kb in range(NB):
                P.emit("dve", dve.scalar_tensor_tensor(
                    out=M[:, kb, 0:1], in0=E["scr2"][:, kb, 0:1],
                    scalar=V[:, kb, vidx:vidx + 1],
                    in1=E["xnprev"][:, kb, :], op0=ALU.mult, op1=ALU.add))
                P.emit("dve", dve.scalar_tensor_tensor(
                    out=M[:, kb, 1:wk_], in0=E["scr2"][:, kb, 1:wk_],
                    scalar=V[:, kb, vidx:vidx + 1],
                    in1=E["xt"][:, kb, 0:wk_ - 1], op0=ALU.mult, op1=ALU.add))
        return P.mark("dve")

    P.wait("dve", st.tok_sq_war)
    tok_kx = emit_mix("mixa", 2, st.toks_bank.get("mixa"))
    tok_vx = emit_mix("mixb", 3, st.toks_bank.get("mixb"))
    if red:
        P.emit("dve", dve.tensor_copy(E["xnprev"][:], E["xt"][:, :, wx - 1:wx]))
        st.tok_xn_prev = P.mark("dve")

    wsrc = {"k": ("wk_b", "wk"), "v": ("wv_b", "wv"), "r": ("wr_b", "wr")}
    half_slots = ["wsA", "wsB"]
    gidx = 0

    def mm_groups(mat, mixt, tok_mix, evac_fn):
        nonlocal gidx
        bnm, cnm = wsrc[mat]
        for hdo in (0, 1):
            slot = half_slots[gidx % 2]
            gidx += 1
            tok_w = _load_wgrp(P, E, st, slot, E[bnm][:, hdo], P.tok_cast[cnm])
            P.wait("pe", tok_w)
            P.wait("pe", tok_mix)
            for do_i in range(4):
                do = 4 * hdo + do_i
                pban = E["PA"] if do % 2 == 0 else E["PD"]
                P.wait("pe", st.toks_bank.get(pban.name))
                for kb in range(NB):
                    P.emit("pe", pe.matmul(pban[:, 0:wk_], E[slot][:, kb, do_i, :],
                                           E[mixt][:, kb, 0:wk_],
                                           start=(kb == 0), stop=(kb == NB - 1)))
                tok_mm = P.mark("pe")
                st.toks_bank[pban.name] = evac_fn(do, pban, tok_mm)
            st.tok_ws_war[slot] = P.mark("pe")
        st.toks_bank[mixt] = P.mark("pe")

    if st.tok_spill is not None:
        P.wait("act", st.tok_spill)
        P.wait("dve", st.tok_spill)
    if st.tok_scanprev is not None:
        P.wait("act", st.tok_scanprev)
        P.wait("dve", st.tok_scanprev)

    toks_ek = [None] * NB

    def evac_k(do, pban, tok_mm):
        P.wait("act", tok_mm)
        if red:
            P.wait("act", st.tok_xn_prev)
        P.emit("act", act.activation(out=Dst_d[:, do, lc:lc + wk_], in_=pban[:, 0:wk_],
                                     func=AF.Exp, bias=E["cm30"][:], scale=1.0))
        toks_ek[do] = P.mark("act")
        return toks_ek[do]

    mm_groups("k", "mixa", tok_kx, evac_k)

    tok_ekmask = None
    if tau == 0:
        mcol = E["cmask"][:, 1:2] if red else E["cmask"][:, 0:1]
        for tk in toks_ek:
            P.wait("dve", tk)
        for do in range(NB):
            P.emit("dve", dve.tensor_scalar(out=Dst_d[:, do, lc:lc + 2],
                                            in0=Dst_d[:, do, lc:lc + 2],
                                            scalar1=mcol, scalar2=None, op0=ALU.mult))
        tok_ekmask = P.mark("dve")

    toks_ekv = [None] * NB

    def evac_v(do, pban, tok_mm):
        P.wait("dve", tok_mm)
        P.wait("dve", toks_ek[do])
        if tok_ekmask is not None:
            P.wait("dve", tok_ekmask)
        P.emit("dve", dve.tensor_mul(Dst_n[:, do, lc:lc + wk_],
                                     Dst_d[:, do, lc:lc + wk_], pban[:, 0:wk_]))
        toks_ekv[do] = P.mark("dve")
        return toks_ekv[do]

    mm_groups("v", "mixb", tok_vx, evac_v)

    if not red:
        tok_rx = emit_mix("mixa", 4, st.toks_bank.get("mixa"))

        def evac_r(do, pban, tok_mm):
            slot = do % 2
            P.wait("act", tok_mm)
            P.wait("act", st.tok_srt.get(slot))
            P.emit("act", act.activation(out=E["srt"][:, slot, 0:wk_], in_=pban[:, 0:wk_],
                                         func=AF.Sigmoid))
            tok_e = P.mark("act")
            P.wait("sp", tok_e)
            st.tok_srt[slot] = P.dma("q_sp", E["sr_hbm"][do, :, ak:bk],
                                     E["srt"][:, slot, 0:wk_])
            return tok_e

        mm_groups("r", "mixa", tok_rx, evac_r)

    if not red:
        P.emit("dve", dve.tensor_copy(E["xnprev"][:], E["xt"][:, :, wx - 1:wx]))
        st.tok_xn_prev = P.mark("dve")
    st.tok_tile_consumed = P.mark("pe")
    st.tok_scr2_dve = P.mark("dve")

    tok_spill_now = None
    if (not red) and tau <= 1:
        P.wait("sp", toks_ekv[NB - 1])
        if tok_ekmask is not None:
            P.wait("sp", tok_ekmask)
        for tk in toks_ek:
            P.wait("sp", tk)
        P.dma("q_sp", E["spill"][0, :, :, ak:bk].rearrange("a p w -> p a w"),
              Dst_d[:, :, lc:lc + wk_])
        tok_spill_now = P.dma("q_sp",
                              E["spill"][1, :, :, ak:bk].rearrange("a p w -> p a w"),
                              Dst_n[:, :, lc:lc + wk_])
        st.tok_spill = tok_spill_now

    P.wait("dve", toks_ekv[NB - 1])
    for tk in toks_ek:
        P.wait("dve", tk)
    if tok_spill_now is not None:
        P.wait("dve", tok_spill_now)
    if st.tok_scanprev is not None:
        P.wait("dve", st.tok_scanprev)
    for do in range(NB):
        ini_d = 0.0 if (tau == 0) else E["statec"][:, do, 0:1]
        ini_n = 0.0 if (tau == 0) else E["statec"][:, do, 1:2]
        P.emit("dve", dve.tensor_tensor_scan(
            out=Dst_d[:, do, lc:lc + wk_],
            data0=E["dvec"][:, do, 2:3].broadcast_to((128, wk_)),
            data1=Dst_d[:, do, lc:lc + wk_],
            initial=ini_d, op0=ALU.mult, op1=ALU.add))
        P.emit("dve", dve.tensor_tensor_scan(
            out=Dst_n[:, do, lc:lc + wk_],
            data0=E["dvec"][:, do, 2:3].broadcast_to((128, wk_)),
            data1=Dst_n[:, do, lc:lc + wk_],
            initial=ini_n, op0=ALU.mult, op1=ALU.add))
    tok_scans = P.mark("dve")
    P.wait("dve", tok_scans)
    P.emit("dve", dve.tensor_copy(E["statec"][:, :, 0:1], Dst_d[:, :, lc + wk_ - 1:lc + wk_]))
    P.emit("dve", dve.tensor_copy(E["statec"][:, :, 1:2], Dst_n[:, :, lc + wk_ - 1:lc + wk_]))
    st.tok_scanprev = P.mark("dve")
    if (not red) and tau == 1:
        P.wait("dve", st.tok_scanprev)
        P.emit("dve", dve.tensor_copy(E["Sd"][:, :, 0:1], E["statec"][:, :, 0:1]))
        P.emit("dve", dve.tensor_copy(E["Sn"][:, :, 0:1], E["statec"][:, :, 1:2]))
        st.tok_scanprev = P.mark("dve")
    if red and tau == 4:
        P.wait("dve", tok_scans)
        P.emit("dve", dve.tensor_copy(E["sumred"][:, 0, :], Dst_d[:, :, lc]))
        P.emit("dve", dve.tensor_copy(E["sumred"][:, 1, :], Dst_n[:, :, lc]))


def _exchange(P, E, tok_sumown, tok_sumred):
    g = P.h["pool"]
    P.wait("pool", tok_sumown)
    P.wait("pool", tok_sumred)
    g.wait_ge(P.sem["q_sp"], 48)
    P.emit("pool", g.memset(E["sflag"][:], 123))
    base = P.cnt["q_gp"]
    with g.register("par") as par, g.register("cnt") as cnt, \
            g.register("flg") as flg, g.register("nd") as nd:
        g.reg_load(par, E["spar"][:, 0:1])
        for slot in (0, 1):
            with g.If_eq(par, slot):
                g.dma_start(out=E["shx"][slot, 0], in_=E["sumown"][:, 0, :]).then_inc(P.sem["q_gp"], 16)
                g.dma_start(out=E["shx"][slot, 1], in_=E["sumown"][:, 1, :]).then_inc(P.sem["q_gp"], 16)
                g.dma_start(out=E["shx"][slot, 2], in_=E["sumred"][:, 0, :]).then_inc(P.sem["q_gp"], 16)
                g.dma_start(out=E["shx"][slot, 3], in_=E["sumred"][:, 1, :]).then_inc(P.sem["q_gp"], 16)
                g.wait_ge(P.sem["q_gp"], base + 64)
                g.dma_start(out=E["shf"][slot:slot + 1, :], in_=E["sflag"][:]).then_inc(P.sem["q_gp"], 16)
        P.cnt["q_gp"] = base + 80
        g.reg_mov(cnt, P.cnt["q_gp"])
        g.memset(E["spoll"][:], 0)
        g.reg_mov(nd, 1)
        with g.While(nd):
            with g.If_eq(par, 0):
                g.dma_start(out=E["spoll"][:], in_=E["shf"][1:2, :]).then_inc(P.sem["q_gp"], 16)
            with g.If_eq(par, 1):
                g.dma_start(out=E["spoll"][:], in_=E["shf"][0:1, :]).then_inc(P.sem["q_gp"], 16)
            g.reg_alu(cnt, cnt, 16, ALU.add)
            g.wait_ge(P.sem["q_gp"], cnt)
            g.reg_load(flg, E["spoll"][0:1, 0:1])
            g.reg_alu(nd, flg, 123, ALU.not_equal)
        with g.If_eq(par, 0):
            g.dma_start(out=E["pairin"][:],
                        in_=E["shx"][1].rearrange("i p b -> p i b")).then_inc(P.sem["q_gp"], 16)
        with g.If_eq(par, 1):
            g.dma_start(out=E["pairin"][:],
                        in_=E["shx"][0].rearrange("i p b -> p i b")).then_inc(P.sem["q_gp"], 16)
        g.reg_alu(cnt, cnt, 16, ALU.add)
        g.wait_ge(P.sem["q_gp"], cnt)
    P.emit("pool", g.engine_nop())
    P.xchg_tok = P.mark("pool")


def _carry(P, E):
    dve, act = P.h["dve"], P.h["act"]
    _real_emit = P.emit

    def emit_se(eng, instr):
        _real_emit(eng, instr)
        if eng == "dve":
            P.wait("dve", P.mark("dve"))
        return instr
    P.emit = emit_se
    P.wait("dve", P.xchg_tok)
    C = E["carry"]
    cs = E["cselt"]
    tmp = E["scr2"]
    P.emit("dve", dve.memset(C[:], 0.0))
    for step in range(3):
        selpo = cs[:, 2 + 3 * step:3 + 3 * step]
        selpr = cs[:, 3 + 3 * step:4 + 3 * step]
        selsr = cs[:, 4 + 3 * step:5 + 3 * step]
        for arr in (0, 1):
            I = tmp[:, arr, 0:NB]
            P.emit("dve", dve.tensor_scalar(out=I, in0=E["pairin"][:, 0 + arr, :],
                                            scalar1=selpo, scalar2=None, op0=ALU.mult))
            P.emit("dve", dve.scalar_tensor_tensor(out=I, in0=E["pairin"][:, 2 + arr, :],
                                                   scalar=selpr, in1=I,
                                                   op0=ALU.mult, op1=ALU.add))
            P.emit("dve", dve.scalar_tensor_tensor(out=I, in0=E["sumred"][:, arr, :],
                                                   scalar=selsr, in1=I,
                                                   op0=ALU.mult, op1=ALU.add))
        if step > 0:
            lencol = cs[:, 10 + step:11 + step]
            P.emit("dve", dve.tensor_scalar(out=tmp[:, 2, 0:NB],
                                            in0=E["dvec"][:, :, 0:1].squeeze(2),
                                            scalar1=lencol, scalar2=-1.0,
                                            op0=ALU.mult, op1=ALU.mult))
            P.wait("act", P.mark("dve"))
            P.emit("act", act.activation(out=tmp[:, 2, 0:NB], in_=tmp[:, 2, 0:NB],
                                         func=AF.Exp))
            P.wait("dve", P.mark("act"))
            for arr in (0, 1):
                P.emit("dve", dve.tensor_mul(C[:, arr, :], C[:, arr, :], tmp[:, 2, 0:NB]))
        for arr in (0, 1):
            P.emit("dve", dve.tensor_add(C[:, arr, :], C[:, arr, :], tmp[:, arr, 0:NB]))
    P.emit = _real_emit
    P.mark("dve")


def _correct_h1(P, E):
    dve, act, pool = P.h["dve"], P.h["act"], P.h["pool"]
    tok_c = P.mark("dve")
    for do in range(NB):
        for (c0, c1) in ((0, 514), (514, HW)):
            w = c1 - c0
            P.wait("pool", tok_c)
            P.emit("pool", pool.iota(E["big"][:, 1, 0:w], pattern=[[1, w]],
                                     base=1023 + c0, channel_multiplier=0,
                                     allow_small_or_imprecise_dtypes=True))
            tok_i = P.mark("pool")
            P.wait("dve", tok_i)
            P.emit("dve", dve.tensor_scalar(out=E["big"][:, 0, 0:w], in0=E["big"][:, 1, 0:w],
                                            scalar1=E["dvec"][:, do, 1:2], scalar2=None,
                                            op0=ALU.mult))
            t1 = P.mark("dve")
            P.wait("act", t1)
            P.emit("act", act.activation(out=E["big"][:, 0, 0:w], in_=E["big"][:, 0, 0:w],
                                         func=AF.Exp))
            t2 = P.mark("act")
            P.wait("dve", t2)
            P.emit("dve", dve.scalar_tensor_tensor(
                out=E["Sd"][:, do, c0:c1], in0=E["big"][:, 0, 0:w],
                scalar=E["carry"][:, 0, do:do + 1], in1=E["Sd"][:, do, c0:c1],
                op0=ALU.mult, op1=ALU.add))
            P.emit("dve", dve.scalar_tensor_tensor(
                out=E["Sn"][:, do, c0:c1], in0=E["big"][:, 0, 0:w],
                scalar=E["carry"][:, 1, do:do + 1], in1=E["Sn"][:, do, c0:c1],
                op0=ALU.mult, op1=ALU.add))
            tok_c = P.mark("dve")
    P.mark("dve")


def _combine(P, E, half):
    dve, act = P.h["dve"], P.h["act"]
    ka, kb = (0, 1023) if half == 0 else (1023, KC)
    w = kb - ka
    tok_S = P.mark("dve")
    tok_aw = {0: None, 1: None}
    tok_srh = {0: None, 1: None}
    for do in range(NB):
        sl = do % 2
        P.wait("sp", tok_srh[sl])
        tok_sr = P.dma("q_sp", E["srh"][:, sl, 0:w], E["sr_hbm"][do, :, ka:kb])
        b = E["big"]
        P.wait("dve", tok_S)
        P.wait("dve", tok_aw[sl])
        P.emit("dve", dve.tensor_scalar(out=b[:, 0, 0:w], in0=E["Sn"][:, do, 1:1 + w],
                                        scalar1=E["dvec"][:, do, 5:6], scalar2=None,
                                        op0=ALU.mult))
        P.emit("dve", dve.scalar_tensor_tensor(
            out=b[:, 0, 0:w], in0=E["Sn"][:, do, 0:w],
            scalar=E["dvec"][:, do, 6:7], in1=b[:, 0, 0:w], op0=ALU.mult, op1=ALU.add))
        P.emit("dve", dve.tensor_scalar(out=b[:, 1, 0:w], in0=E["Sd"][:, do, 1:1 + w],
                                        scalar1=E["dvec"][:, do, 5:6], scalar2=None,
                                        op0=ALU.mult))
        P.emit("dve", dve.scalar_tensor_tensor(
            out=b[:, 1, 0:w], in0=E["Sd"][:, do, 0:w],
            scalar=E["dvec"][:, do, 6:7], in1=b[:, 1, 0:w], op0=ALU.mult, op1=ALU.add))
        t1 = P.mark("dve")
        if half == 1 and "cmb" in E["tap_d"]:
            P.wait("sp", t1)
            P.dma("q_out", E["tap_d"]["cmb"][do, 0, :, 0:w], b[:, 0, 0:w])
            tz = P.dma("q_out", E["tap_d"]["cmb"][do, 1, :, 0:w], b[:, 1, 0:w])
            P.h["sp"].wait_ge(P.sem["q_out"], tz[1])
            P.wait("dve", tz)
            P.wait("act", tz)
        P.wait("act", t1)
        P.emit("act", act.activation(out=b[:, 1, 0:w], in_=b[:, 1, 0:w], func=AF.Ln))
        P.wait("act", P.mark("act"))
        P.emit("act", act.activation(out=b[:, 1, 0:w], in_=b[:, 1, 0:w], func=AF.Exp,
                                     scale=-1.0))
        t3 = P.mark("act")
        if half == 1 and "cmb" in E["tap_d"]:
            P.wait("sp", t3)
            tz = P.dma("q_out", E["tap_d"]["cmb"][do, 2, :, 0:w], b[:, 1, 0:w])
            P.h["sp"].wait_ge(P.sem["q_out"], tz[1])
            P.wait("dve", tz)
            P.wait("act", tz)
        P.wait("dve", t3)
        P.wait("dve", tok_sr)
        P.emit("dve", dve.tensor_mul(b[:, 0, 0:w], b[:, 0, 0:w], b[:, 1, 0:w]))
        P.emit("dve", dve.tensor_mul(E["aw"][:, sl, 0:w], b[:, 0, 0:w],
                                     E["srh"][:, sl, 0:w]))
        tok_a = P.mark("dve")
        tok_srh[sl] = tok_a
        P.wait("sp", tok_a)
        tok_aw[sl] = P.dma("q_sp", E["a_hbm"][do, :, ka:kb], E["aw"][:, sl, 0:w])
    P.tok_sr_done = P.mark("dve")
    P.tok_a_done = tok_aw[1] if tok_aw[1] else tok_aw[0]
    P.tok_a_done_all = [tok_aw[0], tok_aw[1]]


def _h0_scan(P, E):
    dve = P.h["dve"]
    tok_cmb = P.mark("dve")
    P.wait("sp", tok_cmb)
    t1 = P.dma("q_sp", E["Sd"][:, :, 1:1024], E["spill"][0].rearrange("a p w -> p a w"))
    t2 = P.dma("q_sp", E["Sn"][:, :, 1:1024], E["spill"][1].rearrange("a p w -> p a w"))
    P.wait("dve", t1)
    P.wait("dve", t2)
    P.emit("dve", dve.tensor_copy(E["Sd"][:, :, 0], E["carry"][:, 0, :]))
    P.emit("dve", dve.tensor_copy(E["Sn"][:, :, 0], E["carry"][:, 1, :]))
    P.wait("dve", P.mark("dve"))
    for do in range(NB):
        P.emit("dve", dve.tensor_tensor_scan(
            out=E["Sd"][:, do, 1:1024],
            data0=E["dvec"][:, do, 2:3].broadcast_to((128, 1023)),
            data1=E["Sd"][:, do, 1:1024],
            initial=E["Sd"][:, do, 0:1], op0=ALU.mult, op1=ALU.add))
        P.emit("dve", dve.tensor_tensor_scan(
            out=E["Sn"][:, do, 1:1024],
            data0=E["dvec"][:, do, 2:3].broadcast_to((128, 1023)),
            data1=E["Sn"][:, do, 1:1024],
            initial=E["Sn"][:, do, 0:1], op0=ALU.mult, op1=ALU.add))
    P.mark("dve")



def _ph34_tile(P, E, ka, kb):
    import kernel as _K
    pe, dve, act, pool = P.h["pe"], P.h["dve"], P.h["act"], P.h["pool"]
    _base_par = P.paranoid
    P.paranoid = _base_par or ("ph34a" in _PARANOID)
    w = kb - ka
    st = P.st_after_ph1
    V = E["vecs"]
    first = (ka == 0)

    def wload(slot, grp_ap, cast_nm):
        return _load_wgrp(P, E, st, slot, grp_ap, P.tok_cast[cast_nm])

    P.wait("sp", st.tok_tile_consumed)
    P.wait("sp", P.tok_a_done_all[0])
    P.wait("sp", P.tok_a_done_all[1])
    tok_a = P.dma("q_sp", E["at3"][:, :, 0:w],
                  E["a_hbm"][:, :, ka:kb].rearrange("a p w -> p a w"))
    if "at3" in E["tap_d"]:
        P.wait("sp", tok_a)
        tz = P.dma("q_out", E["tap_d"]["at3"][:, :, ka:kb].rearrange("a p w -> p a w"),
                   E["at3"][:, :, 0:w])
        P.h["sp"].wait_ge(P.sem["q_out"], tz[1])
    if st.tok_scanprev is not None:
        P.wait("sp", st.tok_scanprev)
    if getattr(st, "tok_xt_free", None) is not None:
        P.wait("sp", st.tok_xt_free)
    tok_x = P.dma("q_sp", E["xt"][:, :, 0:w],
                  E["xfm"][:, :, ka + 1:kb + 1].rearrange("a p w -> p a w"))

    tok_bc = _bcast(P, E, E["rho_sv"][:, ka + 1:kb + 1], w, 0, None, st.tok_bc_consumed)
    tok_bc2 = _bcast(P, E, E["mrh_sv"][:, ka + 1:kb + 1], w, 512, None, st.tok_bc_consumed)

    P.wait("dve", tok_x)
    P.wait("dve", tok_bc)
    P.wait("dve", tok_bc2)
    for kbb in range(NB):
        P.emit("dve", dve.tensor_mul(E["xt"][:, kbb, 0:w], E["xt"][:, kbb, 0:w],
                                     E["PBC"][:, 0:w]))
        P.emit("dve", dve.tensor_sub(E["xt"][:, kbb, 0:w], E["xt"][:, kbb, 0:w],
                                     E["PBC"][:, 512:512 + w]))
    tok_z1 = P.mark("dve")
    st.tok_bc_consumed = tok_z1
    if "z1t" in E["tap_d"]:
        P.wait("sp", tok_z1)
        tz = P.dma("q_out", E["tap_d"]["z1t"][:, :, ka:kb].rearrange("a p w -> p a w"),
                   E["xt"][:, :, 0:w])
        P.h["sp"].wait_ge(P.sem["q_out"], tz[1])

    gsl = [0]

    def next_slot():
        s = ("wsA", "wsB")[gsl[0] % 2]
        gsl[0] += 1
        return s

    P.wait("pe", tok_a)
    toks_za = [None] * NB
    for hdo in (0, 1):
        slot = next_slot()
        tok_w = wload(slot, E["wo_b"][:, hdo], "wo")
        P.wait("pe", tok_w)
        for do_i in range(4):
            do = 4 * hdo + do_i
            pban = E["PA"] if do % 2 == 0 else E["PD"]
            P.wait("pe", st.toks_bank.get(pban.name))
            for kbb in range(NB):
                P.emit("pe", pe.matmul(pban[:, 0:w], E[slot][:, kbb, do_i, :],
                                       E["at3"][:, kbb, 0:w],
                                       start=(kbb == 0), stop=(kbb == NB - 1)))
            tok_mm = P.mark("pe")
            P.wait("dve", tok_mm)
            P.wait("dve", tok_z1)
            P.emit("dve", dve.tensor_add(E["scr"][:, do, 0:w], E["xt"][:, do, 0:w],
                                         pban[:, 0:w]))
            toks_za[do] = P.mark("dve")
            st.toks_bank[pban.name] = toks_za[do]
        st.tok_ws_war[slot] = P.mark("pe")
    tok_zattn = P.mark("dve")
    if "zattn" in E["tap_d"]:
        P.wait("sp", tok_zattn)
        tz = P.dma("q_out", E["tap_d"]["zattn"][:, :, ka:kb].rearrange("a p w -> p a w"),
                   E["scr"][:, :, 0:w])
        P.h["sp"].wait_ge(P.sem["q_out"], tz[1])

    st.tok_stats_src = tok_zattn
    tok_var = _stats(P, E, E["scr"][:, :, 0:w], w, st)
    R = E["rows"]
    tok_rho = _rsqrt_row(P, E, R[:, 1, 0:w], R[:, 2, 0:w], w, tok_var)
    P.wait("dve", tok_rho)
    P.emit("dve", dve.tensor_mul(R[:, 3, 0:w], R[:, 0, 0:w], R[:, 2, 0:w]))
    tok_mr = P.mark("dve")
    st.tok_rows_prev = tok_mr
    tok_b1 = _bcast(P, E, R[:, 2, :], w, 0, tok_mr, st.tok_bc_consumed)
    tok_b2 = _bcast(P, E, R[:, 3, :], w, 512, tok_mr, st.tok_bc_consumed)
    P.wait("dve", tok_b1)
    P.wait("dve", tok_b2)
    for kbb in range(NB):
        P.emit("dve", dve.tensor_mul(E["xt"][:, kbb, 0:w], E["scr"][:, kbb, 0:w],
                                     E["PBC"][:, 0:w]))
        P.emit("dve", dve.tensor_sub(E["xt"][:, kbb, 0:w], E["xt"][:, kbb, 0:w],
                                     E["PBC"][:, 512:512 + w]))
        P.emit("dve", dve.tensor_scalar(out=E["xt"][:, kbb, 0:w], in0=E["xt"][:, kbb, 0:w],
                                        scalar1=V[:, kbb, 5:6], scalar2=V[:, kbb, 6:7],
                                        op0=ALU.mult, op1=ALU.add))
    if first:
        for kbb in range(NB):
            P.emit("dve", dve.select(out=E["xt"][:, kbb, 0:2],
                                     mask=E["cmaski"][:].broadcast_to((128, 2)),
                                     on_true=E["xt"][:, kbb, 0:2],
                                     on_false=E["zer2"][:]))
    tok_xn2 = P.mark("dve")
    st.tok_bc_consumed = tok_xn2
    if "xn2" in E["tap_d"]:
        P.wait("sp", tok_xn2)
        tz = P.dma("q_out", E["tap_d"]["xn2"][:, :, ka:kb].rearrange("a p w -> p a w"),
                   E["xt"][:, :, 0:w])
        P.h["sp"].wait_ge(P.sem["q_out"], tz[1])

    def shift_mix(Mt, vidx, hcol, war_tok, src):
        if war_tok is not None:
            P.wait("dve", war_tok)
        M = E[Mt]
        P.emit("dve", dve.tensor_sub(E["scr2"][:, :, 0:1], src[:, :, 0:1],
                                     E["hsave"][:, :, hcol:hcol + 1]))
        P.emit("dve", dve.tensor_sub(E["scr2"][:, :, 1:w], src[:, :, 1:w],
                                     src[:, :, 0:w - 1]))
        for kbb in range(NB):
            P.emit("dve", dve.scalar_tensor_tensor(
                out=M[:, kbb, 0:1], in0=E["scr2"][:, kbb, 0:1],
                scalar=V[:, kbb, vidx:vidx + 1],
                in1=E["hsave"][:, kbb, hcol:hcol + 1], op0=ALU.mult, op1=ALU.add))
            P.emit("dve", dve.scalar_tensor_tensor(
                out=M[:, kbb, 1:w], in0=E["scr2"][:, kbb, 1:w],
                scalar=V[:, kbb, vidx:vidx + 1],
                in1=src[:, kbb, 0:w - 1], op0=ALU.mult, op1=ALU.add))
        return P.mark("dve")

    P.wait("dve", st.tok_sq_war)
    tok_fkx = shift_mix("mixa", 7, 0, st.toks_bank.get("mixa"), E["xt"])
    tok_frx = shift_mix("mixb", 8, 0, st.toks_bank.get("mixb"), E["xt"])
    P.emit("dve", dve.tensor_copy(E["hsave"][:, :, 0:1], E["xt"][:, :, w - 1:w]))
    tok_hs = P.mark("dve")

    toks_sfr = [None] * NB
    for hdo in (0, 1):
        slot = next_slot()
        tok_w = wload(slot, E["fwrec_b"][:, hdo], "fwrec")
        P.wait("pe", tok_w)
        P.wait("pe", tok_frx)
        for do_i in range(4):
            do = 4 * hdo + do_i
            pban = E["PB"] if do % 2 == 0 else E["PC"]
            P.wait("pe", st.toks_bank.get(pban.name))
            for kbb in range(NB):
                P.emit("pe", pe.matmul(pban[:, 0:w], E[slot][:, kbb, do_i, :],
                                       E["mixb"][:, kbb, 0:w],
                                       start=(kbb == 0), stop=(kbb == NB - 1)))
            tok_mm = P.mark("pe")
            P.wait("act", tok_mm)
            P.wait("act", st.tok_sq_war)
            P.emit("act", act.activation(out=E["scr2"][:, do, 0:w], in_=pban[:, 0:w],
                                         func=AF.Sigmoid))
            toks_sfr[do] = P.mark("act")
            st.toks_bank[pban.name] = toks_sfr[do]
        st.tok_ws_war[slot] = P.mark("pe")
    st.toks_bank["mixb"] = P.mark("pe")
    tok_sfr = P.mark("act")

    P.paranoid = _base_par or ("ph34b" in _PARANOID)
    P.wait("dve", tok_fkx)
    P.wait("dve", tok_frx)
    P.wait("dve", tok_hs)
    P.emit("dve", dve.memset(E["xt"][:, :, 0:w], 0.0))
    tok_vkz = P.mark("dve")
    toks_vk = [tok_vkz] * NB
    for fq in range(4):
        toks_fka = [None] * NB
        for hfo in (0, 1):
            slot = next_slot()
            tok_w = wload(slot, E["fwk_b"][:, 2 * fq + hfo], "fwk")
            P.wait("pe", tok_w)
            P.wait("pe", tok_fkx)
            for fo_i in range(4):
                fo = 4 * hfo + fo_i
                pban = E["PA"] if fo % 2 == 0 else E["PD"]
                P.wait("pe", st.toks_bank.get(pban.name))
                for kbb in range(NB):
                    P.emit("pe", pe.matmul(pban[:, 0:w], E[slot][:, kbb, fo_i, :],
                                           E["mixa"][:, kbb, 0:w],
                                           start=(kbb == 0), stop=(kbb == NB - 1)))
                tok_mm = P.mark("pe")
                P.wait("act", tok_mm)
                P.wait("act", st.toks_bank.get("fka"))
                P.emit("act", act.activation(out=E["big"][:, 0, 0:w], in_=pban[:, 0:w],
                                             func=AF.Relu))
                tok_rl = P.mark("act")
                P.wait("act", tok_rl)
                P.emit("act", act.activation(out=E["fka"][:, fo, 0:w], in_=E["big"][:, 0, 0:w],
                                             func=AF.Square))
                toks_fka[fo] = P.mark("act")
                st.toks_bank[pban.name] = toks_fka[fo]
            st.tok_ws_war[slot] = P.mark("pe")
        for hdo in (0, 1):
            slot = next_slot()
            tok_w = wload(slot, E["fwv_b"][:, fq, hdo], "fwv")
            P.wait("pe", tok_w)
            for tk in toks_fka:
                P.wait("pe", tk)
            for do_i in range(4):
                do = 4 * hdo + do_i
                pban = E["PB"] if do % 2 == 0 else E["PC"]
                P.wait("pe", st.toks_bank.get(pban.name))
                for kbb in range(NB):
                    P.emit("pe", pe.matmul(pban[:, 0:w], E[slot][:, kbb, do_i, :],
                                           E["fka"][:, kbb, 0:w],
                                           start=(kbb == 0), stop=(kbb == NB - 1)))
                tok_mm = P.mark("pe")
                P.wait("dve", tok_mm)
                P.wait("dve", toks_vk[do])
                P.emit("dve", dve.tensor_add(E["xt"][:, do, 0:w], E["xt"][:, do, 0:w],
                                             pban[:, 0:w]))
                toks_vk[do] = P.mark("dve")
                st.toks_bank[pban.name] = toks_vk[do]
            st.tok_ws_war[slot] = P.mark("pe")
        st.toks_bank["fka"] = P.mark("pe")

    for do in range(NB):
        P.wait("dve", toks_vk[do])
        P.wait("dve", toks_sfr[do])
        P.emit("dve", dve.tensor_mul(E["xt"][:, do, 0:w], E["xt"][:, do, 0:w],
                                     E["scr2"][:, do, 0:w]))
        P.emit("dve", dve.tensor_add(E["scr"][:, do, 0:w], E["scr"][:, do, 0:w],
                                     E["xt"][:, do, 0:w]))
    tok_za2 = P.mark("dve")
    if "za2" in E["tap_d"]:
        P.wait("sp", tok_za2)
        tz = P.dma("q_out", E["tap_d"]["za2"][:, :, ka:kb].rearrange("a p w -> p a w"),
                   E["scr"][:, :, 0:w])
        P.h["sp"].wait_ge(P.sem["q_out"], tz[1])

    P.paranoid = _base_par or ("ph34c" in _PARANOID)
    st.tok_stats_src = tok_za2
    st.tok_sq_war = tok_za2
    tok_var = _stats(P, E, E["scr"][:, :, 0:w], w, st)
    tok_rho = _rsqrt_row(P, E, R[:, 1, 0:w], R[:, 2, 0:w], w, tok_var)
    P.wait("dve", tok_rho)
    P.emit("dve", dve.tensor_mul(R[:, 3, 0:w], R[:, 0, 0:w], R[:, 2, 0:w]))
    tok_mr = P.mark("dve")
    st.tok_rows_prev = tok_mr
    tok_b1 = _bcast(P, E, R[:, 2, :], w, 0, tok_mr, st.tok_bc_consumed)
    tok_b2 = _bcast(P, E, R[:, 3, :], w, 512, tok_mr, st.tok_bc_consumed)
    P.wait("sp", P.mark("dve"))
    tok_x2 = P.dma("q_sp", E["xt"][:, :, 0:w],
                   E["xfm"][:, :, ka + 1:kb + 1].rearrange("a p w -> p a w"))
    P.wait("dve", tok_b1)
    P.wait("dve", tok_b2)
    P.wait("dve", tok_x2)
    for kbb in range(NB):
        P.emit("dve", dve.tensor_mul(E["big"][:, 0, 0:w], E["scr"][:, kbb, 0:w],
                                     E["PBC"][:, 0:w]))
        P.emit("dve", dve.tensor_sub(E["big"][:, 0, 0:w], E["big"][:, 0, 0:w],
                                     E["PBC"][:, 512:512 + w]))
        P.emit("dve", dve.tensor_add(E["xt"][:, kbb, 0:w], E["xt"][:, kbb, 0:w],
                                     E["big"][:, 0, 0:w]))
    tok_x2v = P.mark("dve")
    st.tok_bc_consumed = tok_x2v
    if "x2t" in E["tap_d"]:
        P.wait("sp", tok_x2v)
        tz = P.dma("q_out", E["tap_d"]["x2t"][:, :, ka:kb].rearrange("a p w -> p a w"),
                   E["xt"][:, :, 0:w])
        P.h["sp"].wait_ge(P.sem["q_out"], tz[1])

    st.tok_stats_src = tok_x2v
    tok_var = _stats(P, E, E["xt"][:, :, 0:w], w, st)
    tok_rho = _rsqrt_row(P, E, R[:, 1, 0:w], R[:, 2, 0:w], w, tok_var)
    P.wait("dve", tok_rho)
    P.emit("dve", dve.tensor_mul(R[:, 3, 0:w], R[:, 0, 0:w], R[:, 2, 0:w]))
    tok_mr = P.mark("dve")
    st.tok_rows_prev = tok_mr
    tok_b1 = _bcast(P, E, R[:, 2, :], w, 0, tok_mr, st.tok_bc_consumed)
    tok_b2 = _bcast(P, E, R[:, 3, :], w, 512, tok_mr, st.tok_bc_consumed)
    P.wait("dve", tok_b1)
    P.wait("dve", tok_b2)
    P.wait("dve", tok_za2)
    for kbb in range(NB):
        P.emit("dve", dve.tensor_mul(E["scr"][:, kbb, 0:w], E["xt"][:, kbb, 0:w],
                                     E["PBC"][:, 0:w]))
        P.emit("dve", dve.tensor_sub(E["scr"][:, kbb, 0:w], E["scr"][:, kbb, 0:w],
                                     E["PBC"][:, 512:512 + w]))
    if first:
        for kbb in range(NB):
            P.emit("dve", dve.select(out=E["scr"][:, kbb, 0:2],
                                     mask=E["cmaski"][:].broadcast_to((128, 2)),
                                     on_true=E["scr"][:, kbb, 0:2],
                                     on_false=E["zer2"][:]))
    tok_z2 = P.mark("dve")
    st.tok_bc_consumed = tok_z2

    P.wait("dve", st.tok_sq_war)
    tok_xr = shift_mix("mixa", 11, 1, st.toks_bank.get("mixa"), E["scr"])
    tok_xk = shift_mix("mixb", 12, 1, st.toks_bank.get("mixb"), E["scr"])
    P.emit("dve", dve.tensor_copy(E["hsave"][:, :, 1:2], E["scr"][:, :, w - 1:w]))
    tok_hs2 = P.mark("dve")

    P.paranoid = _base_par or ("ph34d" in _PARANOID)
    toks_r2 = [None] * NB
    for hdo in (0, 1):
        slot = next_slot()
        tok_w = wload(slot, E["cwr_b"][:, hdo], "cwr")
        P.wait("pe", tok_w)
        P.wait("pe", tok_xr)
        for do_i in range(4):
            do = 4 * hdo + do_i
            pban = E["PB"] if do % 2 == 0 else E["PC"]
            P.wait("pe", st.toks_bank.get(pban.name))
            for kbb in range(NB):
                P.emit("pe", pe.matmul(pban[:, 0:w], E[slot][:, kbb, do_i, :],
                                       E["mixa"][:, kbb, 0:w],
                                       start=(kbb == 0), stop=(kbb == NB - 1)))
            tok_mm = P.mark("pe")
            P.wait("act", tok_mm)
            P.wait("act", st.tok_sq_war)
            P.emit("act", act.activation(out=E["scr2"][:, do, 0:w], in_=pban[:, 0:w],
                                         func=AF.Sigmoid))
            toks_r2[do] = P.mark("act")
            st.toks_bank[pban.name] = toks_r2[do]
        st.tok_ws_war[slot] = P.mark("pe")
    st.toks_bank["mixa"] = P.mark("pe")

    toks_kk = [None] * NB
    for hdo in (0, 1):
        slot = next_slot()
        tok_w = wload(slot, E["cwk_b"][:, hdo], "cwk")
        P.wait("pe", tok_w)
        P.wait("pe", tok_xk)
        P.wait("pe", st.toks_bank.get("fka"))
        for do_i in range(4):
            do = 4 * hdo + do_i
            pban = E["PA"] if do % 2 == 0 else E["PD"]
            P.wait("pe", st.toks_bank.get(pban.name))
            for kbb in range(NB):
                P.emit("pe", pe.matmul(pban[:, 0:w], E[slot][:, kbb, do_i, :],
                                       E["mixb"][:, kbb, 0:w],
                                       start=(kbb == 0), stop=(kbb == NB - 1)))
            tok_mm = P.mark("pe")
            P.wait("act", tok_mm)
            P.emit("act", act.activation(out=E["big"][:, 0, 0:w], in_=pban[:, 0:w],
                                         func=AF.Relu))
            tok_rl = P.mark("act")
            P.wait("act", tok_rl)
            P.emit("act", act.activation(out=E["fka"][:, do, 0:w], in_=E["big"][:, 0, 0:w],
                                         func=AF.Square))
            toks_kk[do] = P.mark("act")
            st.toks_bank[pban.name] = toks_kk[do]
        st.tok_ws_war[slot] = P.mark("pe")
    st.toks_bank["mixb"] = P.mark("pe")

    toks_y2 = [None] * NB
    for hdo in (0, 1):
        slot = next_slot()
        tok_w = wload(slot, E["cwv_b"][:, hdo], "cwv")
        P.wait("pe", tok_w)
        for tk in toks_kk:
            P.wait("pe", tk)
        for do_i in range(4):
            do = 4 * hdo + do_i
            pban = E["PB"] if do % 2 == 0 else E["PC"]
            P.wait("pe", st.toks_bank.get(pban.name))
            for kbb in range(NB):
                P.emit("pe", pe.matmul(pban[:, 0:w], E[slot][:, kbb, do_i, :],
                                       E["fka"][:, kbb, 0:w],
                                       start=(kbb == 0), stop=(kbb == NB - 1)))
            tok_mm = P.mark("pe")
            P.wait("dve", tok_mm)
            P.wait("dve", toks_r2[do])
            P.wait("dve", tok_xk)
            P.wait("dve", tok_hs2)
            P.emit("dve", dve.tensor_mul(E["scr"][:, do, 0:w], E["scr2"][:, do, 0:w],
                                         pban[:, 0:w]))
            toks_y2[do] = P.mark("dve")
            st.toks_bank[pban.name] = toks_y2[do]
        st.tok_ws_war[slot] = P.mark("pe")
    st.toks_bank["fka"] = P.mark("pe")
    tok_y2 = P.mark("dve")

    st.tok_stats_src = tok_y2
    st.tok_sq_war = tok_y2
    tok_var = _stats(P, E, E["scr"][:, :, 0:w], w, st)
    tok_rho = _rsqrt_row(P, E, R[:, 1, 0:w], R[:, 2, 0:w], w, tok_var)
    P.wait("dve", tok_rho)
    P.emit("dve", dve.tensor_mul(R[:, 3, 0:w], R[:, 0, 0:w], R[:, 2, 0:w]))
    tok_mr = P.mark("dve")
    st.tok_rows_prev = tok_mr
    tok_b1 = _bcast(P, E, R[:, 2, :], w, 0, tok_mr, st.tok_bc_consumed)
    tok_b2 = _bcast(P, E, R[:, 3, :], w, 512, tok_mr, st.tok_bc_consumed)
    P.wait("dve", tok_b1)
    P.wait("dve", tok_b2)
    P.wait("dve", st.tok_sq_war)
    for kbb in range(NB):
        P.emit("dve", dve.tensor_mul(E["scr2"][:, kbb, 0:w], E["scr"][:, kbb, 0:w],
                                     E["PBC"][:, 0:w]))
        P.emit("dve", dve.tensor_sub(E["scr2"][:, kbb, 0:w], E["scr2"][:, kbb, 0:w],
                                     E["PBC"][:, 512:512 + w]))
        P.emit("dve", dve.tensor_add(E["scr2"][:, kbb, 0:w], E["scr2"][:, kbb, 0:w],
                                     E["xt"][:, kbb, 0:w]))
    tok_out = P.mark("dve")
    st.tok_bc_consumed = tok_out

    P.paranoid = _base_par
    oa = max(ka, 2)
    P.wait("sp", tok_out)
    tok_odma = P.dma("q_out", E["out_d"][:, :, oa - 2:kb - 2].rearrange("a p w -> p a w"),
                     E["scr2"][:, :, oa - ka:w])
    st.tok_tile_consumed = P.mark("pe")
    st.tok_scanprev = None
    st.tok_xt_free = tok_out
    st.tok_sq_war = tok_odma
    st.tok_stats_src = None



def kernel(**inp):
    key = ("prog", tuple(sorted(_TAPS)))
    if key not in _CACHE:
        _CACHE[key] = build_program(_TAPS)
    nc, tap_d = _CACHE[key]

    x = np.asarray(inp["L0_attn_x"])
    B, T, _ = x.shape

    def vec(name):
        return np.asarray(inp[name], np.float32).reshape(-1)

    def fm(v):
        return np.ascontiguousarray(v.reshape(NB, 128).T)

    vcols = [vec("L0_attn_ln1_w"), vec("L0_attn_ln1_b"),
             vec("L0_attn_time_mix_k"), vec("L0_attn_time_mix_v"),
             vec("L0_attn_time_mix_r"),
             vec("L0_attn_ln2_w"), vec("L0_attn_ln2_b"),
             vec("L0_attn_f_time_mix_k"), vec("L0_attn_f_time_mix_r"),
             vec("L0_attn_time_decay"), vec("L0_attn_time_first"),
             vec("L0_ffn_mu_r"), vec("L0_ffn_mu_k")]
    vpk = np.stack([fm(v) for v in vcols], axis=2).astype(np.float32)

    wm = {
        "vpk": vpk,
        "wk": np.ascontiguousarray(np.asarray(inp["L0_attn_w_key"], np.float32)),
        "wv": np.ascontiguousarray(np.asarray(inp["L0_attn_w_value"], np.float32)),
        "wr": np.ascontiguousarray(np.asarray(inp["L0_attn_w_recept"], np.float32)),
        "wo": np.ascontiguousarray(np.asarray(inp["L0_attn_w_output"], np.float32)),
        "fwk": np.ascontiguousarray(np.asarray(inp["L0_attn_f_w_key"], np.float32)),
        "fwrec": np.ascontiguousarray(np.asarray(inp["L0_attn_f_w_recept"], np.float32)),
        "fwv": np.ascontiguousarray(np.asarray(inp["L0_attn_f_w_value"], np.float32)),
        "cwr": np.ascontiguousarray(np.asarray(inp["L0_ffn_W_r"], np.float32).T),
        "cwk": np.ascontiguousarray(np.asarray(inp["L0_ffn_W_k"], np.float32).T),
        "cwv": np.ascontiguousarray(np.asarray(inp["L0_ffn_W_v"], np.float32).T),
    }

    def slice_fm(b, t0):
        lo = t0 - 3
        pad = max(0, -lo)
        sl = x[b, max(0, lo):t0 + TBLK, :]
        arr = np.zeros((XC, D), np.float32)
        arr[pad:] = sl
        return np.ascontiguousarray(arr.T).reshape(NB, 128, XC)

    in_maps = []
    for c in range(8):
        b, q = c // 4, c % 4
        t0 = q * TBLK
        m = dict(wm)
        m["xfm"] = slice_fm(b, t0)
        if q == 2:
            m["xred"] = slice_fm(b, 0)
            mask_redh = 0.0
        elif q == 3:
            m["xred"] = slice_fm(b, TBLK)
            mask_redh = 1.0
        else:
            m["xred"] = np.zeros((NB, 128, XC), np.float32)
            mask_redh = 0.0
        sel = np.zeros(16, np.float32)
        sel[0] = 0.0 if q == 0 else 1.0
        sel[1] = mask_redh
        if q == 1:
            sel[2] = 1.0
        elif q == 2:
            sel[4] = 1.0
            sel[6] = 1.0
            sel[11] = TBLK
        elif q == 3:
            sel[3] = 1.0
            sel[7] = 1.0
            sel[11] = TBLK
            sel[8] = 1.0
            sel[12] = TBLK
        m["csel"] = np.ascontiguousarray(np.broadcast_to(sel, (128, 16))).astype(np.float32)
        m["cpar"] = np.array([[c % 2, 0]], np.int32)
        in_maps.append(m)

    res = run_bass_kernel_spmd(nc, in_maps, core_ids=list(range(8)))

    out = np.zeros((B, T, D), np.float32)
    for c in range(8):
        b, q = c // 4, c % 4
        r = np.asarray(res.results[c]["out"])
        out[b, q * TBLK:(q + 1) * TBLK, :] = r.reshape(D, TBLK).T
    if _TAPS:
        kernel.tap_results = [res.results[c] for c in range(8)]
    return out
